# version 01
# baseline (speedup 1.0000x reference)
import math
import numpy as np
import ml_dtypes
from contextlib import ExitStack

import concourse.bass as bass
import concourse.mybir as mybir
from concourse.bass_utils import run_bass_kernel_spmd

F32 = mybir.dt.float32
BF16 = mybir.dt.bfloat16
AF = mybir.ActivationFunctionType
ALU = mybir.AluOpType
AX = mybir.AxisListType

S = 2048
E = 1024
HID = 4096
NT = S // 128
NE = E // 128
NF = 3 * E // 128
NJ = HID // 128
NPAIR = 8
MASKV = -10000000.0
SHUF32 = [x for i in range(16) for x in (2 * i + 1, 2 * i)]

ENGS = ("sync", "tensor", "vector", "scalar", "gpsimd")


class Sem:
    __slots__ = ("h", "n", "key")

    def __init__(self, h, key):
        self.h = h
        self.n = 0
        self.key = key


class Walk:

    def __init__(self, active, eng, sems):
        self.active = active
        self.eng = eng
        self.sems = sems
        self.reached = {e: {} for e in ENGS}

    def op(self, eng, emit, incs=()):
        if self.active == eng:
            ins = emit(self.eng[eng])
            for s, k in incs:
                ins.then_inc(s.h, k)
        for s, k in incs:
            s.n += k

    def dma(self, eng, out, in_, sem, transpose=False, allow_nc=False):
        if transpose:
            self.op(eng, lambda e: e.dma_start_transpose(out, in_),
                    incs=[(sem, 16)])
        else:
            self.op(eng, lambda e: e.dma_start(
                out=out, in_=in_, allow_slow_non_contiguous=allow_nc),
                incs=[(sem, 16)])

    def drain(self, eng):
        self.op(eng, lambda e: e.drain())

    def wait(self, eng, sem, val=None):
        v = sem.n if val is None else val
        if v <= self.reached[eng].get(sem.key, 0):
            return
        self.reached[eng][sem.key] = v
        if self.active == eng:
            self.eng[eng].wait_ge(sem.h, v)


def build(dbg=(), phases=("A", "B", "C"), probe=False, dumpB=False,
          dumpC=False, b_tiles=NT):
    nc = bass.Bass("TRN2", target_bir_lowering=False)

    x_in = nc.dram_tensor("x", [S, E], F32, kind="ExternalInput")
    wqkvT = nc.dram_tensor("wqkvT", [E, 3 * E], F32, kind="ExternalInput")
    woutT = nc.dram_tensor("woutT", [E, E], F32, kind="ExternalInput")
    wfcT = nc.dram_tensor("wfcT", [E, HID], F32, kind="ExternalInput")
    wprojT = nc.dram_tensor("wprojT", [HID, E], F32, kind="ExternalInput")
    lnw_in = nc.dram_tensor("ln_w", [HID], F32, kind="ExternalInput")
    lnb_in = nc.dram_tensor("ln_b", [HID], F32, kind="ExternalInput")
    cos_hi = nc.dram_tensor("cos_hi", [128, S], F32, kind="ExternalInput")
    cos_lo = nc.dram_tensor("cos_lo", [128, S], F32, kind="ExternalInput")
    sin_hi = nc.dram_tensor("sin_hi", [128, S], F32, kind="ExternalInput")
    sin_lo = nc.dram_tensor("sin_lo", [128, S], F32, kind="ExternalInput")
    ident_b = nc.dram_tensor("ident_b", [128, 128], BF16, kind="ExternalInput")
    ident_f = nc.dram_tensor("ident_f", [128, 128], F32, kind="ExternalInput")
    mask_hi_in = nc.dram_tensor("mask_hi", [128, 1024], BF16, kind="ExternalInput")

    y_out = nc.dram_tensor("y", [S, E], F32, kind="ExternalOutput")

    scr = {}
    scr["wq_hi"] = nc.dram_tensor("wq_scr_hi", [NF, NE, 128, 128], BF16)
    scr["wq_lo"] = nc.dram_tensor("wq_scr_lo", [NF, NE, 128, 128], BF16)
    scr["wo_hi"] = nc.dram_tensor("wo_scr_hi", [NE, NE, 128, 128], BF16)
    scr["wo_lo"] = nc.dram_tensor("wo_scr_lo", [NE, NE, 128, 128], BF16)
    scr["wqfc"] = nc.dram_tensor("wqfc_scr", [NE, 128, HID], BF16)
    scr["wqproj"] = nc.dram_tensor("wqproj_scr", [NJ, 128, E], BF16)
    scr["k_hi"] = nc.dram_tensor("k_scr_hi", [NT, NPAIR, 128, 128], BF16)
    scr["k_lo"] = nc.dram_tensor("k_scr_lo", [NT, NPAIR, 128, 128], BF16)
    scr["q_hi"] = nc.dram_tensor("q_scr_hi", [NT, NPAIR, 128, 128], BF16)
    scr["q_lo"] = nc.dram_tensor("q_scr_lo", [NT, NPAIR, 128, 128], BF16)
    scr["v_hi"] = nc.dram_tensor("v_scr_hi", [NT, E, 128], BF16)
    scr["v_lo"] = nc.dram_tensor("v_scr_lo", [NT, E, 128], BF16)
    scr["xp"] = nc.dram_tensor("x_scr", [S, E], F32)

    dumpC_outs = {}
    if dumpC:
        for nm, cols, dt_ in (("dmc_h", 8192, F32), ("dmc_x2h", 8192, BF16),
                              ("dmc_x2l", 8192, BF16), ("dmc_mu", 256, F32),
                              ("dmc_rb", 256, F32), ("dmc_xnt", 2048, BF16),
                              ("dmc_xntl", 2048, BF16), ("dmc_y", 2048, F32)):
            dumpC_outs[nm] = nc.dram_tensor(nm, [128, cols], dt_,
                                            kind="ExternalOutput")
    dumpB_outs = {}
    if dumpB:
        for nm, cols, dt_ in (("dmp_psnap", 2048, BF16),
                              ("dmp_gacc", 128, F32),
                              ("dmp_s", 4096, F32), ("dmp_phi", 4096, BF16),
                              ("dmp_plo", 4096, BF16), ("dmp_atnh", 2048, BF16),
                              ("dmp_atnl", 2048, BF16), ("dmp_scal", 16, F32),
                              ("dmp_xb", 2048, F32), ("dmp_q", 2048, BF16),
                              ("dmp_k", 16384, BF16), ("dmp_v", 16384, BF16)):
            dumpB_outs[nm] = nc.dram_tensor(nm, [128, cols], dt_,
                                            kind="ExternalOutput")
    dbg_probe = None
    dbg_probe2 = None
    if probe:
        dbg_probe2 = nc.dram_tensor("dbg_probe2", [128 * 2048 + 128 * 32], F32,
                                    kind="ExternalOutput")
    if probe:
        dbg_probe = nc.dram_tensor("dbg_probe", [4, 128, 4096], F32,
                                   kind="ExternalOutput")
    dbg_outs = {}
    for name in dbg:
        t = scr[name]
        dbg_outs[name] = nc.dram_tensor("dbg_" + name, list(t.shape), t.dtype,
                                        kind="ExternalOutput")

    es = ExitStack()
    esAB = ExitStack()
    esA = ExitStack()
    esB = ExitStack()
    esC = ExitStack()

    def mk_sb(stack):
        def _sb(name, cols, dt=F32):
            return stack.enter_context(nc.sbuf_tensor(name, [128, cols], dt))
        return _sb

    sbP, sbAB, sbA, sbB, sbC = (mk_sb(s) for s in (es, esAB, esA, esB, esC))

    id_b = sbP("id_b", 128, BF16)
    id_f = sbP("id_f", 128, F32)
    mask_sb = sbP("mask_sb", 1024, BF16)
    lnw_sb = sbP("lnw_sb", NJ)
    lnb_sb = sbP("lnb_sb", NJ)
    c15 = sbP("c15", 1)
    onesrow = sbP("onesrow", 128)
    onescol = sbP("onescol", 1)
    tauf = sbP("tauf", 2)
    taup = sbP("taup", 2)
    lnsc = sbP("lnsc", 16)
    gq_in = sbAB("gq_in", 2 * 512)
    gq_cmp = sbAB("gq_cmp", 512, BF16)
    gq_out = sbAB("gq_out", 2 * 512, BF16)
    gacc = sbAB("gacc", 128)
    gred = sbAB("gred", 4)
    cosW = sbA("cosW", S)
    sinW = sbA("sinW", S)
    stage = sbA("stageA", 2 * 3072)
    sphi = sbA("sphiA", 2 * 3072, BF16)
    splo = sbA("sploA", 2 * 3072, BF16)
    x_sb = sbA("x_sbA", 2 * 1024)
    xsq = sbA("xsqA", 1024)
    xc_sb = sbA("xcA", 1024)
    xnh = sbA("xnhA", 1024, BF16)
    xnl = sbA("xnlA", 1024, BF16)
    xnTh = sbA("xnThA", NE * 512, BF16)
    xnTl = sbA("xnTlA", NE * 512, BF16)
    wqf = sbA("wqfA", 2 * 2048, BF16)
    qf = sbA("qfA", 2 * 512)
    shtmp = sbA("shtmpA", 512)
    rtmpA = sbA("rtmpA", 512)
    rtmpB = sbA("rtmpB", 512)
    qr = sbA("qrA", 512)
    snap_sb = sbA("snap_sb", 2048)
    snap2_sb = sbA("snap2_sb", 32)
    esA.close()
    kst_h = sbB("kst_h", NT * 1024, BF16)
    kst_l = sbB("kst_l", NT * 1024, BF16)
    vst_h = sbB("vst_h", NT * 1024, BF16)
    vst_l = sbB("vst_l", NT * 1024, BF16)
    s_sb = sbB("s_sbB", 2 * 2048)
    phi = sbB("phiB", 2 * 2048, BF16)
    plo = sbB("ploB", 2 * 2048, BF16)
    ptsb = sbB("ptsbB", 3 * 512, BF16)
    qbuf = sbB("qbufB", 2048, BF16)
    atnh = sbB("atnhB", NPAIR * 256, BF16)
    atnl = sbB("atnlB", NPAIR * 256, BF16)
    if dumpB:
        wof = sbB("wofB", 2 * 2048, BF16)
        psnap = wof
    else:
        wof = sbB("wofB", 2 * 2048, BF16)
        psnap = None
    xB = sbB("xB", 2048)
    o2 = sbB("o2B", 2 * 256)
    esB.close()
    wqfc_sb = sbC("wqfc_sb", NE * HID, BF16)
    wpj = sbC("wpjC", 2 * 1024, BF16)
    h_sb = sbC("h_sbC", NJ * 256)
    hb_sb = sbC("hb_sbC", NJ * 256, BF16)
    hb2_sb = sbC("hb2C", NJ * 256, BF16)
    xpC = sbC("xpC", 2048)
    yC = sbC("yC", 2048)
    xnThC = sbC("xnThC", NE * 256, BF16)
    xnTlC = sbC("xnTlC", NE * 256, BF16)
    xcC = sbC("xcC", 1024)
    xnhC = sbC("xnhC", 1024, BF16)
    xnlC = sbC("xnlC", 1024, BF16)
    xsqC = sbC("xsqC", 1024)
    mubc = sbC("mubc", 256)
    rbc = sbC("rbc", 256)
    aA = sbC("aAC", 256)
    aB = sbC("aBC", 256)
    o2c = sbC("o2cC", 2 * 256)
    onescol_b = sbC("onescol_b", 1, BF16)
    esC.close()

    pbig = es.enter_context(nc.psum_tensor("pbig", [128, 2048], F32))

    class _BankView:
        def __init__(self, base):
            self.base = base

        def __getitem__(self, idx):
            rows, cols = idx
            start = cols.start or 0
            stop = cols.stop
            return pbig[rows, self.base + start:self.base + stop]

    pb = [_BankView(i * 512) for i in range(4)]
    pt = [es.enter_context(nc.psum_tensor(f"pt{i}", [128, 1024], BF16))
          for i in range(2)]
    pv = [es.enter_context(nc.psum_tensor(f"pv{i}", [128, 512], F32))
          for i in range(2)]

    sem_names = [
        "dx0", "dx1", "dw0", "dw1", "dwq0", "dwq1", "dscr",
        "dgq0", "dgq1", "dgw", "dtab",
        "dkv", "dq0", "dq1", "dwo0", "dwo1", "dxb0", "dxb1", "dxp",
        "dc0", "dc1", "dcw", "dpj0", "dpj1", "dy",
        "pe", "dve", "act", "gp",
    ]
    sem_hs = {k: es.enter_context(nc.semaphore(k)) for k in sem_names}

    def emit_program(w, sm):
        pe, dve, act, gp = sm["pe"], sm["dve"], sm["act"], sm["gp"]

        def settle_b(bank, col=512):
            w.op("tensor", lambda e, bank=bank, col=col: e.transpose(
                pt[bank][:, col:col + 128], id_b[:], id_b[:]),
                incs=[(pe, 1)])

        def settle_f(bank_ap):
            w.op("tensor", lambda e, bank_ap=bank_ap: e.transpose(
                bank_ap, id_f[:], id_f[:]), incs=[(pe, 1)])

        w.dma("sync", cosW[:], cos_hi[:, :], sm["dtab"])
        w.dma("sync", sinW[:], sin_hi[:, :], sm["dtab"])
        w.dma("sync", mask_sb[:], mask_hi_in[:, :], sm["dtab"])
        w.dma("sync", id_b[:], ident_b[:, :], sm["dtab"])
        w.dma("sync", id_f[:], ident_f[:, :], sm["dtab"])
        w.dma("sync", lnw_sb[:], bass.AP(lnw_in, 0, [[1, 128], [128, NJ]]),
              sm["dtab"], allow_nc=True)
        w.dma("sync", lnb_sb[:], bass.AP(lnb_in, 0, [[1, 128], [128, NJ]]),
              sm["dtab"], allow_nc=True)
        w.dma("sync", stage[:, 0:2048], cos_lo[:, :], sm["dtab"])
        w.dma("sync", stage[:, 2048:4096], sin_lo[:, :], sm["dtab"])
        w.op("vector", lambda e: e.memset(c15[:], 1.5))
        w.op("vector", lambda e: e.memset(onescol[:], 1.0))
        w.op("vector", lambda e: e.memset(onesrow[0:1, :], 1.0))
        w.wait("vector", sm["dtab"])
        w.op("vector", lambda e: e.tensor_add(cosW[:], cosW[:], stage[:, 0:2048]))
        w.op("vector", lambda e: e.tensor_add(sinW[:], sinW[:], stage[:, 2048:4096]),
             incs=[(dve, 1)])
        tables_done = dve.n

        last_split = {0: tables_done, 1: tables_done}
        last_store = {0: 0, 1: 0}

        def split_store(idx, width, src_ap, dsts):
            par = idx % 2
            o = par * 3072
            stg = stage[:, o:o + width]
            hi = sphi[:, o:o + width]
            lo = splo[:, o:o + width]
            w.wait("sync", dve, last_split[par])
            w.dma("sync", stg, src_ap, sm["dw%d" % par])
            w.wait("vector", sm["dw%d" % par])
            w.wait("vector", sm["dscr"], last_store[par])
            w.op("vector", lambda e: e.tensor_copy(hi, stg), incs=[(dve, 1)])
            w.op("vector", lambda e: e.scalar_tensor_tensor(
                lo, stg, 1.0, hi, op0=ALU.mult, op1=ALU.subtract),
                incs=[(dve, 1)])
            last_split[par] = dve.n
            w.wait("sync", dve)
            w.dma("sync", dsts[0], hi, sm["dscr"])
            w.dma("sync", dsts[1], lo, sm["dscr"])
            last_store[par] = sm["dscr"].n

        for ec in range(NE):
            src = bass.AP(wqkvT, ec * 128 * 3072, [[3072, 128], [1, 3072]])
            dst_hi = bass.AP(scr["wq_hi"], ec * 16384,
                             [[128, 128], [NE * 16384, NF], [1, 128]])
            dst_lo = bass.AP(scr["wq_lo"], ec * 16384,
                             [[128, 128], [NE * 16384, NF], [1, 128]])
            split_store(ec, 3072, src, (dst_hi, dst_lo))
        for dc in range(NE):
            src = bass.AP(woutT, dc * 128 * 1024, [[1024, 128], [1, 1024]])
            dst_hi = bass.AP(scr["wo_hi"], dc * 16384,
                             [[128, 128], [NE * 16384, NE], [1, 128]])
            dst_lo = bass.AP(scr["wo_lo"], dc * 16384,
                             [[128, 128], [NE * 16384, NE], [1, 128]])
            split_store(NE + dc, 1024, src, (dst_hi, dst_lo))
        startup_scr = sm["dscr"].n

        gq_last = {0: 0, 1: 0}
        gq_chunks = []
        for ec in range(NE):
            for qt in range(8):
                gq_chunks.append((bass.AP(wfcT, ec * 128 * HID + qt * 512,
                                          [[HID, 128], [1, 512]]),
                                  512, ec * 8 + qt))
        for hc in range(NJ):
            for qt in range(2):
                gq_chunks.append((bass.AP(wprojT, hc * 128 * E + qt * 512,
                                          [[E, 128], [1, 512]]),
                                  512, 64 + hc * 2 + qt))
        gq_state = {"issued": 0, "reduced": 0}
        gq_load_val = {}

        def gq_tick():
            while (gq_state["issued"] < len(gq_chunks)
                   and gq_state["issued"] - gq_state["reduced"] < 2):
                i = gq_state["issued"]
                src_ap, ncols, _ = gq_chunks[i]
                par = i % 2
                w.wait("gpsimd", act, gq_last[par])
                w.dma("gpsimd", gq_in[:, par * 512:par * 512 + ncols],
                      src_ap, sm["dgq%d" % par])
                gq_load_val[i] = sm["dgq%d" % par].n
                gq_state["issued"] += 1
            if gq_state["reduced"] < gq_state["issued"]:
                i = gq_state["reduced"]
                _, ncols, col = gq_chunks[i]
                par = i % 2
                o = par * 512
                w.wait("scalar", sm["dgq%d" % par], gq_load_val[i])
                w.op("scalar", lambda e, o=o, ncols=ncols, col=col:
                     e.activation(gq_cmp[:, 0:ncols], gq_in[:, o:o + ncols],
                                  AF.Abs, accum_out=gacc[:, col:col + 1]),
                     incs=[(act, 1)])
                gq_last[par] = act.n
                gq_state["reduced"] += 1

        def gq_drain():
            while gq_state["reduced"] < len(gq_chunks):
                gq_tick()

        xslot_free = {0: 0, 1: 0}
        wq_last_mm = {0: 0, 1: 0}
        qk_bank_free = {0: 0, 1: 0}
        tslot_free = [(act, 0), (dve, 0)]
        qf_split_done = {0: tables_done, 1: tables_done}
        split_stage_free = {0: startup_scr, 1: startup_scr}
        xn_trans_done = 0

        if "A" in phases:
            for sup in range(4):
                for tl in range(4):
                    t = sup * 4 + tl
                    par = t % 2
                    xo = par * 1024
                    w.wait("sync", dve, xslot_free[par])
                    w.dma("sync", x_sb[:, xo:xo + 1024],
                          bass.AP(x_in, t * 128 * E, [[E, 128], [1, E]]),
                          sm["dx%d" % par])
                    xt = x_sb[:, xo:xo + 1024]
                    w.wait("vector", sm["dx%d" % par])
                    w.wait("scalar", sm["dx%d" % par])
                    (s1, s2, mu, e2c, msq, var, uu, sq, r0, e1, ee2, ee3, rr) = \
                        [lnsc[:, i:i + 1] for i in range(13)]
                    w.op("vector", lambda e: e.reduce_sum(s1, xt, AX.X))
                    w.op("scalar", lambda e: e.activation(
                        xsq[:], xt, AF.Square, accum_out=s2), incs=[(act, 1)])
                    w.drain("vector")
                    w.op("vector", lambda e: e.tensor_scalar_mul(mu, s1, 1.0 / 1024.0))
                    w.wait("vector", act)
                    w.op("vector", lambda e: e.tensor_scalar_mul(e2c, s2, 1.0 / 1024.0))
                    w.drain("vector")
                    w.op("vector", lambda e: e.tensor_mul(msq, mu, mu))
                    w.drain("vector")
                    w.op("vector", lambda e: e.tensor_sub(var, e2c, msq))
                    w.drain("vector")
                    w.op("vector", lambda e: e.tensor_scalar_add(uu, var, 1e-5),
                         incs=[(dve, 1)])
                    w.wait("scalar", dve)
                    w.op("scalar", lambda e: e.activation(sq, uu, AF.Sqrt),
                         incs=[(act, 1)])
                    w.wait("vector", act)
                    w.op("vector", lambda e: e.reciprocal(r0, sq))
                    w.drain("vector")
                    w.op("vector", lambda e: e.tensor_mul(e1, r0, r0))
                    w.drain("vector")
                    w.op("vector", lambda e: e.tensor_mul(ee2, uu, e1))
                    w.drain("vector")
                    w.op("vector", lambda e: e.scalar_tensor_tensor(
                        ee3, ee2, -0.5, c15[:], op0=ALU.mult, op1=ALU.add))
                    w.drain("vector")
                    w.op("vector", lambda e: e.tensor_mul(rr, r0, ee3),
                         incs=[(dve, 1)])
                    w.wait("vector", dve)
                    w.op("vector", lambda e: e.tensor_scalar(
                        xc_sb[:], xt, mu, rr, op0=ALU.subtract, op1=ALU.mult),
                        incs=[(dve, 1)])
                    xslot_free[par] = dve.n
                    w.wait("vector", pe, xn_trans_done)
                    w.op("vector", lambda e: e.tensor_copy(xnh[:], xc_sb[:]),
                         incs=[(dve, 1)])
                    w.op("vector", lambda e: e.scalar_tensor_tensor(
                        xnl[:], xc_sb[:], 1.0, xnh[:], op0=ALU.mult,
                        op1=ALU.subtract), incs=[(dve, 1)])
                    split_done = dve.n
                    pend = None

                    def xn_evac(item):
                        i, bk, dst = item
                        evn = "scalar" if bk == 0 else "vector"
                        evs = act if bk == 0 else dve
                        w.wait(evn, pe)
                        if evn == "scalar":
                            w.op("scalar", lambda e, dst=dst, bk=bk:
                                 e.copy(dst, pt[bk][:, 0:128]), incs=[(act, 1)])
                        else:
                            w.op("vector", lambda e, dst=dst, bk=bk:
                                 e.tensor_copy(dst, pt[bk][:, 0:128]),
                                 incs=[(dve, 1)])
                        tslot_free[bk] = (evs, evs.n)

                    for i in range(16):
                        ec, half = i // 2, i % 2
                        srcb = xnh if half == 0 else xnl
                        dstb = xnTh if half == 0 else xnTl
                        bk = i % 2
                        fs, fc_ = tslot_free[bk]
                        w.wait("tensor", dve, split_done)
                        w.wait("tensor", fs, fc_)
                        w.op("tensor", lambda e, bk=bk, srcb=srcb, ec=ec:
                             e.transpose(pt[bk][:, 0:128],
                                         srcb[:, ec * 128:(ec + 1) * 128],
                                         id_b[:]), incs=[(pe, 1)])
                        if pend is not None:
                            xn_evac(pend)
                        dst = dstb[:, ec * 512 + tl * 128:ec * 512 + tl * 128 + 128]
                        pend = (i, bk, dst)
                    settle_b(pend[1])
                    xn_evac(pend)
                    xn_trans_done = pe.n
                    if dbg_probe2 is not None and t < 2:
                        w.op("vector", lambda e, t=t: e.tensor_copy(
                            snap_sb[:, t * 1024:(t + 1) * 1024], xc_sb[:]))
                        w.op("vector", lambda e, t=t: e.tensor_copy(
                            snap2_sb[:, t * 16:(t + 1) * 16], lnsc[:]),
                            incs=[(dve, 1)])

                if dbg_probe is not None and sup < 2:
                    w.wait("gpsimd", act)
                    w.wait("gpsimd", dve)
                    w.dma("gpsimd", bass.AP(dbg_probe, sup * 2 * 128 * 4096,
                                            [[4096, 128], [1, 4096]]),
                          xnTh[:], sm["dy"])
                    w.dma("gpsimd", bass.AP(dbg_probe, (sup * 2 + 1) * 128 * 4096,
                                            [[4096, 128], [1, 4096]]),
                          xnTl[:], sm["dy"])

                for f in range(NF):
                    par = f % 2
                    wo = par * 2048
                    w.wait("sync", pe, wq_last_mm[par])
                    w.wait("sync", sm["dscr"], startup_scr)
                    src_hi = bass.AP(scr["wq_hi"], f * NE * 16384,
                                     [[128, 128], [16384, NE], [1, 128]])
                    src_lo = bass.AP(scr["wq_lo"], f * NE * 16384,
                                     [[128, 128], [16384, NE], [1, 128]])
                    w.dma("sync", wqf[:, wo:wo + 1024], src_hi, sm["dwq%d" % par])
                    w.dma("sync", wqf[:, wo + 1024:wo + 2048], src_lo,
                          sm["dwq%d" % par])
                    bank = pb[par]
                    w.wait("tensor", sm["dwq%d" % par])
                    w.wait("tensor", act)
                    w.wait("tensor", dve)
                    w.wait("tensor", act, qk_bank_free[par])
                    nmm = 0
                    for xsrc, wcol in ((xnTh, 0), (xnTl, 0), (xnTh, 1024)):
                        for ec in range(NE):
                            first, last = nmm == 0, nmm == 23
                            w.op("tensor", lambda e, xsrc=xsrc, wcol=wcol,
                                 ec=ec, first=first, last=last, wo=wo:
                                 e.matmul(bank[:, 0:512],
                                          wqf[:, wo + wcol + ec * 128:
                                              wo + wcol + (ec + 1) * 128],
                                          xsrc[:, ec * 512:(ec + 1) * 512],
                                          start=first, stop=last),
                                 incs=([(pe, 1)] if last else []))
                            nmm += 1
                    wq_last_mm[par] = pe.n
                    settle_b(0)
                    qfo = par * 512
                    qfa = qf[:, qfo:qfo + 512]
                    w.wait("scalar", pe)
                    w.wait("scalar", dve, qf_split_done[par])
                    w.op("scalar", lambda e, qfa=qfa: e.copy(qfa, bank[:, 0:512]),
                         incs=[(act, 1)])
                    qk_bank_free[par] = act.n
                    w.wait("vector", act)
                    if f < 16:
                        cs = cosW[:, sup * 512:(sup + 1) * 512]
                        sn = sinW[:, sup * 512:(sup + 1) * 512]
                        w.op("vector", lambda e, qfa=qfa: e.stream_shuffle(
                            shtmp[:], qfa, SHUF32))
                        w.op("vector", lambda e, qfa=qfa, cs=cs: e.tensor_mul(
                            rtmpA[:], qfa, cs))
                        w.op("vector", lambda e, sn=sn: e.tensor_mul(
                            rtmpB[:], shtmp[:], sn))
                        w.op("vector", lambda e: e.tensor_add(
                            qr[:], rtmpA[:], rtmpB[:]))
                        ssrc = qr[:, 0:512]
                    else:
                        ssrc = qfa
                    sso = par * 512
                    hi_ap = sphi[:, sso:sso + 512]
                    lo_ap = splo[:, sso:sso + 512]
                    w.wait("vector", sm["dscr"], split_stage_free[par])
                    w.op("vector", lambda e, hi_ap=hi_ap, ssrc=ssrc:
                         e.tensor_copy(hi_ap, ssrc), incs=[(dve, 1)])
                    w.op("vector", lambda e, lo_ap=lo_ap, hi_ap=hi_ap, ssrc=ssrc:
                         e.scalar_tensor_tensor(lo_ap, ssrc, 1.0, hi_ap,
                                                op0=ALU.mult, op1=ALU.subtract),
                         incs=[(dve, 1)])
                    qf_split_done[par] = dve.n
                    w.wait("sync", dve)
                    if f < 8:
                        base = (sup * 4 * NPAIR + f) * 16384
                        dims = [[128, 128], [NPAIR * 16384, 4], [1, 128]]
                        tgt = ("q_hi", "q_lo")
                    elif f < 16:
                        base = (sup * 4 * NPAIR + (f - 8)) * 16384
                        dims = [[128, 128], [NPAIR * 16384, 4], [1, 128]]
                        tgt = ("k_hi", "k_lo")
                    else:
                        base = sup * 4 * (E * 128) + (f - 16) * 128 * 128
                        dims = [[128, 128], [E * 128, 4], [1, 128]]
                        tgt = ("v_hi", "v_lo")
                    w.dma("sync", bass.AP(scr[tgt[0]], base, dims), hi_ap,
                          sm["dscr"])
                    w.dma("sync", bass.AP(scr[tgt[1]], base, dims), lo_ap,
                          sm["dscr"])
                    split_stage_free[par] = sm["dscr"].n
                    gq_tick()

        gq_drain()
        a_scr_done = sm["dscr"].n

        w.wait("tensor", act)
        w.wait("tensor", dve, tables_done)
        w.op("tensor", lambda e: e.matmul(pv[0][0:1, 0:64], onescol[:],
                                          gacc[:, 0:64], start=True, stop=True))
        w.op("tensor", lambda e: e.matmul(pv[0][0:1, 64:128], onescol[:],
                                          gacc[:, 64:128], start=True, stop=True),
             incs=[(pe, 1)])
        settle_f(pv[0][:, 256:384])
        w.wait("vector", pe)
        w.op("vector", lambda e: e.reduce_sum(gred[0:1, 0:1], pv[0][0:1, 0:64],
                                              AX.X))
        w.op("vector", lambda e: e.reduce_sum(gred[0:1, 1:2], pv[0][0:1, 64:128],
                                              AX.X))
        w.drain("vector")
        for col in range(2):
            w.op("vector", lambda e, col=col: e.tensor_scalar(
                gred[0:1, 2 + col:3 + col], gred[0:1, col:col + 1],
                1.0 / (HID * E), 1e-6, op0=ALU.mult, op1=ALU.add))
        w.drain("vector")
        w.op("vector", lambda e: e.tensor_scalar_mul(gred[0:1, 0:1],
                                                     gred[0:1, 2:3], 0.5),
             incs=[(dve, 1)])
        w.op("vector", lambda e: e.tensor_scalar_mul(gred[0:1, 1:2],
                                                     gred[0:1, 3:4], 0.5),
             incs=[(dve, 1)])
        w.wait("tensor", dve)
        w.op("tensor", lambda e: e.matmul(pv[1][:, 0:2], onesrow[0:1, :],
                                          gred[0:1, 0:2], start=True, stop=True),
             incs=[(pe, 1)])
        settle_f(pv[1][:, 256:384])
        w.wait("vector", pe)
        w.op("vector", lambda e: e.tensor_copy(tauf[:, 0:1], pv[1][:, 0:1]),
             incs=[(dve, 1)])
        w.op("vector", lambda e: e.tensor_copy(taup[:, 0:1], pv[1][:, 1:2]),
             incs=[(dve, 1)])
        w.drain("vector")
        w.op("vector", lambda e: e.tensor_scalar_mul(tauf[:, 1:2], tauf[:, 0:1],
                                                     -1.0), incs=[(dve, 1)])
        w.op("vector", lambda e: e.tensor_scalar_mul(taup[:, 1:2], taup[:, 0:1],
                                                     -1.0), incs=[(dve, 1)])
        tau_done = dve.n

        w.wait("gpsimd", dve, tau_done)
        gq2_last = {0: 0, 1: 0}
        gq2_pos_store = {0: 0, 1: 0}
        ngc2 = [0]

        def gq_quant(src_ap, ncols, tau_sb, dst_ap):
            par = ngc2[0] % 2
            o = par * 512
            w.wait("gpsimd", gp, gq2_last[par])
            w.dma("gpsimd", gq_in[:, o:o + ncols], src_ap, sm["dgq%d" % par])
            w.wait("gpsimd", sm["dgq%d" % par])
            w.wait("gpsimd", sm["dgw"], gq2_pos_store[par])
            w.op("gpsimd", lambda e: e.tensor_scalar(
                gq_cmp[:, 0:ncols], gq_in[:, o:o + ncols], tau_sb[:, 0:1],
                None, op0=ALU.is_gt), incs=[(gp, 1)])
            w.op("gpsimd", lambda e: e.tensor_scalar(
                gq_out[:, o:o + ncols], gq_in[:, o:o + ncols], tau_sb[:, 1:2],
                -1.0, op0=ALU.is_lt, op1=ALU.mult), incs=[(gp, 1)])
            gq2_last[par] = gp.n
            w.dma("gpsimd", dst_ap, gq_cmp[:, 0:ncols], sm["dgw"])
            pos_done = sm["dgw"].n
            gq2_pos_store[par] = pos_done
            w.wait("gpsimd", sm["dgw"], pos_done)
            w.op("gpsimd", lambda e: e.dma_start(
                out=dst_ap, in_=gq_out[:, o:o + ncols], accum_op=ALU.add),
                incs=[(sm["dgw"], 16)])
            ngc2[0] += 1

        for ec in range(NE):
            for qt in range(8):
                gq_quant(bass.AP(wfcT, ec * 128 * HID + qt * 512,
                                 [[HID, 128], [1, 512]]),
                         512, tauf,
                         bass.AP(scr["wqfc"], ec * 128 * HID + qt * 512,
                                 [[HID, 128], [1, 512]]))
        for hc in range(NJ):
            for qt in range(2):
                gq_quant(bass.AP(wprojT, hc * 128 * E + qt * 512,
                                 [[E, 128], [1, 512]]),
                         512, taup,
                         bass.AP(scr["wqproj"], hc * 128 * E + qt * 512,
                                 [[E, 128], [1, 512]]))
        gq_all = sm["dgw"].n

        def wqfB_ap(col):
            return wof[:, col:col + 1024]

        if "B" in phases:
            for engn in ("sync", "tensor", "vector", "scalar"):
                w.wait(engn, pe)
                w.wait(engn, dve)
                w.wait(engn, act)
                for sname in ("dscr", "dx0", "dx1", "dwq0", "dwq1",
                              "dw0", "dw1", "dtab"):
                    w.wait(engn, sm[sname])

            bank_free = {i: (act, 0) for i in range(4)}
            pt_free = [(act, 0), (act, 0)]
            ptsb_free = [0, 0, 0]
            pv_free = [(act, 0), (act, 0)]
            tile_pe_done = {}
            wof_last = {0: 0, 1: 0}
            phi_free = 0
            atn_free = 0
            kv_cnt = {}
            xB_load = {}
            mx = [lnsc[:, 0:1], lnsc[:, 1:2]]
            mxc = [lnsc[:, 2:3], lnsc[:, 3:4]]
            biasb = [lnsc[:, 4:5], lnsc[:, 5:6]]
            den = [lnsc[:, 6:7], lnsc[:, 7:8]]
            rden = [lnsc[:, 8:9], lnsc[:, 9:10]]

            def issue_kv_loads(t):
                w.wait("sync", pe, tile_pe_done.get(t - 2, 0))
                ksem = sm["dq%d" % (t % 2)]
                for ksrc, kdst in ((scr["k_hi"], kst_h), (scr["k_lo"], kst_l)):
                    srcap = bass.AP(ksrc, t * NPAIR * 16384,
                                    [[128, 128], [16384, NPAIR], [1, 128]])
                    dstap = bass.AP(kdst, t * 128,
                                    [[NT * 1024, 128], [2048, NPAIR], [1, 128]])
                    w.dma("sync", dstap, srcap, ksem)
                for vsrc, vdst in ((scr["v_hi"], vst_h), (scr["v_lo"], vst_l)):
                    srcap = bass.AP(vsrc, t * E * 128, [[128, E], [1, 128]])
                    w.dma("sync", vdst[:, t * 1024:(t + 1) * 1024], srcap,
                          ksem, transpose=True)
                kv_cnt[t] = ksem.n

            def issue_q_load(t):
                w.wait("sync", pe, tile_pe_done.get(t - 1, 0))
                ksem = sm["dq%d" % (t % 2)]
                for qsrc, qcol in ((scr["q_hi"], 0), (scr["q_lo"], 1024)):
                    srcap = bass.AP(qsrc, t * NPAIR * 16384,
                                    [[128, 128], [16384, NPAIR], [1, 128]])
                    dstap = bass.AP(qbuf, qcol,
                                    [[2048, 128], [128, NPAIR], [1, 128]])
                    w.dma("sync", dstap, srcap, ksem)
                kv_cnt[t] = ksem.n

            def issue_x_load(sup):
                par = sup % 2
                t0 = sup * 2
                w.wait("sync", sm["dxp"])
                srcap = bass.AP(x_in, t0 * 128 * E,
                                [[E, 128], [128 * E, 2], [1, E]])
                dstap = bass.AP(xB, 0, [[2048, 128], [1024, 2], [1, 1024]])
                w.dma("sync", dstap, srcap, sm["dxb%d" % par])
                xB_load[sup] = sm["dxb%d" % par].n

            issue_kv_loads(0)
            issue_q_load(0)
            issue_x_load(0)

            for t in range(b_tiles):
                if t + 1 < b_tiles:
                    issue_kv_loads(t + 1)
                par = t % 2
                tl = t % 2
                ksem_t = sm["dq%d" % (t % 2)]
                nk = (t + 1) * 128
                nch = (nk + 511) // 512
                for hp in range(NPAIR):
                    qh_c = hp * 128
                    ql_c = 1024 + hp * 128
                    for c in range(nch):
                        wdt = min(512, nk - 512 * c)
                        for h in range(2):
                            bnk = 2 * (c % 2) + h
                            bs, bc = bank_free[bnk]
                            w.wait("tensor", bs, bc)
                            w.wait("tensor", ksem_t, kv_cnt[t])
                            prow = slice(h * 64, h * 64 + 64)
                            kcol = hp * 2048 + c * 512
                            passes = ((qh_c, kst_h), (ql_c, kst_h), (qh_c, kst_l))
                            for ip, (qc, kstx) in enumerate(passes):
                                w.op("tensor", lambda e, qc=qc, kstx=kstx,
                                     prow=prow, kcol=kcol, wdt=wdt, bnk=bnk,
                                     ip=ip:
                                     e.matmul(pb[bnk][:, 0:wdt],
                                              qbuf[prow, qc:qc + 128],
                                              kstx[prow, kcol:kcol + wdt],
                                              start=(ip == 0), stop=(ip == 2)),
                                     incs=([(pe, 1)] if ip == 2 else []))
                        settle_b(0)
                        w.wait("vector", pe)
                        for h in range(2):
                            bnk = 2 * (c % 2) + h
                            if c == 0:
                                w.op("vector", lambda e, bnk=bnk, wdt=wdt, h=h:
                                     e.reduce_max(mx[h], pb[bnk][:, 0:wdt],
                                                  AX.X))
                            else:
                                w.op("vector", lambda e, bnk=bnk, wdt=wdt, h=h:
                                     e.reduce_max(mxc[h], pb[bnk][:, 0:wdt],
                                                  AX.X))
                                w.drain("vector")
                                w.op("vector", lambda e, h=h: e.tensor_max(
                                    mx[h], mx[h], mxc[h]))
                        if c == nch - 1:
                            o = 128 * t - 512 * c
                            w.drain("vector")
                            for h in range(2):
                                bnk = 2 * (c % 2) + h
                                sl = s_sb[:, h * 2048 + c * 512:
                                          h * 2048 + c * 512 + wdt]
                                msl = mask_sb[:, 512 - o:512 - o + wdt]
                                w.op("vector", lambda e, sl=sl, msl=msl,
                                     bnk=bnk, wdt=wdt:
                                     e.scalar_tensor_tensor(
                                         sl, pb[bnk][:, 0:wdt], 1.0, msl,
                                         op0=ALU.mult, op1=ALU.add),
                                     incs=[(dve, 1)])
                                bank_free[bnk] = (dve, dve.n)
                        else:
                            w.op("vector", lambda e: e.engine_nop(),
                                 incs=[(dve, 1)])
                            w.wait("scalar", dve)
                            for h in range(2):
                                bnk = 2 * (c % 2) + h
                                dst = s_sb[:, h * 2048 + c * 512:
                                           h * 2048 + c * 512 + wdt]
                                w.op("scalar", lambda e, dst=dst, bnk=bnk,
                                     wdt=wdt:
                                     e.copy(dst, pb[bnk][:, 0:wdt]),
                                     incs=[(act, 1)])
                                bank_free[bnk] = (act, act.n)
                    w.drain("vector")
                    for h in range(2):
                        w.op("vector", lambda e, h=h: e.tensor_scalar_mul(
                            biasb[h], mx[h], -0.125), incs=[(dve, 1)])
                    den2 = [lnsc[:, 10:11], lnsc[:, 11:12]]
                    if nk <= 1024:
                        blocks = [(0, 0, nk, 0, den[0]),
                                  (1, 0, nk, 1, den[1])]
                    else:
                        blocks = [(0, 0, 1024, 0, den[0]),
                                  (0, 1024, nk - 1024, 1, den2[0]),
                                  (1, 0, 1024, 0, den[1]),
                                  (1, 1024, nk - 1024, 1, den2[1])]
                    quad_free = {0: 0, 1: 0}
                    exp_cnt = {}

                    def emit_exp(h, cs, wd, q, dacc):
                        w.wait("scalar", dve, quad_free[q])
                        w.wait("scalar", dve)
                        w.op("scalar", lambda e, h=h, cs=cs, wd=wd, q=q,
                             dacc=dacc: e.activation(
                                 pbig[:, q * 1024:q * 1024 + wd],
                                 s_sb[:, h * 2048 + cs:h * 2048 + cs + wd],
                                 AF.Exp, bias=biasb[h], scale=0.125,
                                 accum_out=dacc), incs=[(act, 1)])
                        exp_cnt[(h, cs)] = act.n

                    def emit_den_recip(h, two):
                        if two:
                            w.wait("vector", act, exp_cnt[(h, 1024)])
                            w.op("vector", lambda e, h=h: e.tensor_add(
                                den[h], den[h], den2[h]))
                            w.drain("vector")
                        else:
                            w.wait("vector", act, exp_cnt[(h, 0)])
                        w.op("vector", lambda e, h=h: e.reciprocal(
                            rden[h], den[h]), incs=[(dve, 1)])
                        w.wait("vector", dve)

                    def emit_split(h, cs, wd, q):
                        w.wait("vector", act, exp_cnt[(h, cs)])
                        w.op("vector", lambda e, h=h, cs=cs, wd=wd, q=q:
                             e.tensor_scalar(
                                 phi[:, h * 2048 + cs:h * 2048 + cs + wd],
                                 pbig[:, q * 1024:q * 1024 + wd],
                                 rden[h], None, op0=ALU.mult),
                             incs=[(dve, 1)])
                        w.op("vector", lambda e, h=h, cs=cs, wd=wd, q=q:
                             e.scalar_tensor_tensor(
                                 plo[:, h * 2048 + cs:h * 2048 + cs + wd],
                                 pbig[:, q * 1024:q * 1024 + wd],
                                 rden[h],
                                 phi[:, h * 2048 + cs:h * 2048 + cs + wd],
                                 op0=ALU.mult, op1=ALU.subtract),
                             incs=[(dve, 1)])
                        quad_free[q] = dve.n

                    w.wait("vector", pe, phi_free)
                    if nk <= 1024:
                        emit_exp(0, 0, nk, 0, den[0])
                        emit_exp(1, 0, nk, 1, den[1])
                        emit_den_recip(0, False)
                        emit_split(0, 0, nk, 0)
                        emit_den_recip(1, False)
                        emit_split(1, 0, nk, 1)
                    else:
                        emit_exp(0, 0, 1024, 0, den[0])
                        emit_exp(0, 1024, nk - 1024, 1, den2[0])
                        emit_den_recip(0, True)
                        emit_split(0, 0, 1024, 0)
                        emit_exp(1, 0, 1024, 0, den[1])
                        emit_split(0, 1024, nk - 1024, 1)
                        emit_exp(1, 1024, nk - 1024, 1, den2[1])
                        emit_den_recip(1, True)
                        emit_split(1, 0, 1024, 0)
                        emit_split(1, 1024, nk - 1024, 1)
                    for b in range(4):
                        bank_free[b] = (dve, dve.n)
                    split_cnt = dve.n
                    if dumpB and t == 0:
                        w.op("vector", lambda e, hp=hp: e.tensor_copy(
                            gacc[:, hp * 16:hp * 16 + 16], lnsc[:]),
                            incs=[(dve, 1)])
                        if hp < 4:
                            for j, (buf, hh) in enumerate(((phi, 0), (plo, 0),
                                                           (phi, 1), (plo, 1))):
                                w.op("vector", lambda e, buf=buf, hh=hh, j=j,
                                     hp=hp: e.tensor_copy(
                                         gq_in[:, hp * 256 + j * 64:
                                               hp * 256 + j * 64 + 64].handle
                                         if False else
                                         psnap[:, (hp * 4 + j) * 128:
                                               (hp * 4 + j) * 128 + 128],
                                         buf[:, hh * 2048:hh * 2048 + 128]),
                                     incs=[(dve, 1)])
                        split_cnt = dve.n
                    pvs, pvc = pv_free[0]
                    w.wait("tensor", pvs, pvc)
                    pvs, pvc = pv_free[1]
                    w.wait("tensor", pvs, pvc)

                    def emit_T(k, with_settle=False):
                        bk = k % 2
                        del with_settle
                        fs, fc_ = pt_free[bk]
                        w.wait("tensor", fs, fc_)
                        w.wait("tensor", dve, split_cnt)
                        for i, (h, srcb) in enumerate(
                                ((0, phi), (0, plo), (1, phi), (1, plo))):
                            w.op("tensor", lambda e, bk=bk, h=h, srcb=srcb,
                                 i=i, k=k:
                                 e.transpose(pt[bk][:, i * 128:(i + 1) * 128],
                                             srcb[:, h * 2048 + k * 128:
                                                  h * 2048 + (k + 1) * 128],
                                             id_b[:]),
                                 incs=([(pe, 1)] if i == 3 else []))

                    def emit_ptsb_evac(k):
                        bk = k % 2
                        sl3 = k % 3
                        w.wait("scalar", pe)
                        w.wait("scalar", pe, ptsb_free[sl3])
                        w.op("scalar", lambda e, bk=bk, sl3=sl3: e.copy(
                            ptsb[:, sl3 * 512:(sl3 + 1) * 512],
                            pt[bk][:, 0:512]), incs=[(act, 1)])
                        pt_free[bk] = (act, act.n)

                    def emit_AV(k):
                        bk = k % 3
                        w.wait("tensor", act)
                        for h in range(2):
                            for ip in range(3):
                                vc0 = k * 1024 + hp * 128 + h * 64
                                po = bk * 512 + h * 256
                                prow = slice(h * 64, h * 64 + 64)
                                vstx, pcol = ((vst_h, po), (vst_l, po),
                                              (vst_h, po + 128))[ip]
                                st_ = (k == 0 and ip == 0)
                                sp_ = (k == t_cur and ip == 2)
                                w.op("tensor", lambda e, vstx=vstx, pcol=pcol,
                                     vc0=vc0, h=h, prow=prow, st_=st_, sp_=sp_:
                                     e.matmul(
                                         pv[h][prow, 0:128],
                                         vstx[:, vc0:vc0 + 64],
                                         ptsb[:, pcol:pcol + 128],
                                         start=st_, stop=sp_,
                                         tile_position=(0, h * 64)),
                                     incs=([(pe, 1)] if ip == 2 else []))
                        ptsb_free[bk] = pe.n

                    t_cur = t
                    emit_T(0)
                    for k in range(1, t + 1):
                        emit_T(k)
                        emit_ptsb_evac(k - 1)
                        if k >= 2:
                            emit_AV(k - 2)
                    settle_b((t + 1) % 2)
                    emit_ptsb_evac(t)
                    if t >= 1:
                        emit_AV(t - 1)
                    else:
                        settle_b(0)
                    emit_AV(t)
                    settle_f(pv[0][:, 256:384])
                    settle_f(pv[1][:, 256:384])
                    phi_free = pe.n
                    w.wait("scalar", pe)
                    w.wait("scalar", pe, atn_free)
                    w.wait("vector", pe)
                    w.wait("vector", pe, atn_free)
                    dcol = hp * 256 + tl * 128
                    for h in (1, 0):
                        prow = slice(h * 64, h * 64 + 64)
                        w.op("scalar", lambda e, h=h, prow=prow, dcol=dcol:
                             e.copy(atnh[prow, dcol:dcol + 128],
                                    pv[h][prow, 0:128]), incs=[(act, 1)])
                    w.wait("vector", act)
                    for h in range(2):
                        prow = slice(h * 64, h * 64 + 64)
                        w.op("vector", lambda e, h=h, prow=prow, dcol=dcol:
                             e.scalar_tensor_tensor(
                                 atnl[prow, dcol:dcol + 128],
                                 pv[h][prow, 0:128], 1.0,
                                 atnh[prow, dcol:dcol + 128],
                                 op0=ALU.mult, op1=ALU.subtract),
                             incs=[(dve, 1)])
                    pv_free[0] = (dve, dve.n)
                    pv_free[1] = (dve, dve.n)
                tile_pe_done[t] = pe.n
                if t + 1 < b_tiles:
                    issue_q_load(t + 1)

                if t % 2 == 1 and not dumpB:
                    sup = t // 2
                    atn_h_cnt = act.n
                    atn_l_cnt = dve.n
                    tb = 0
                    for f in range(NE):
                        fpar = f % 2
                        wo_c = fpar * 2048
                        w.wait("sync", pe, wof_last[fpar])
                        for wsrc, wcol in ((scr["wo_hi"], 0),
                                           (scr["wo_lo"], 1024)):
                            srcap = bass.AP(wsrc, f * NE * 16384,
                                            [[128, 128], [16384, NE], [1, 128]])
                            w.dma("sync", wqfB_ap(wo_c + wcol), srcap,
                                  sm["dwo%d" % fpar])
                        bs, bc = bank_free[fpar]
                        w.wait("tensor", bs, bc)
                        w.wait("tensor", sm["dwo%d" % fpar])
                        w.wait("tensor", act, atn_h_cnt)
                        w.wait("tensor", dve, atn_l_cnt)
                        nmm = 0
                        for asrc, wcol in ((atnh, 0), (atnl, 0), (atnh, 1024)):
                            for d in range(NE):
                                first, last = nmm == 0, nmm == 23
                                w.op("tensor", lambda e, asrc=asrc, wcol=wcol,
                                     d=d, first=first, last=last, wo_c=wo_c,
                                     fpar=fpar:
                                     e.matmul(pb[fpar][:, 0:256],
                                              wof[:, wo_c + wcol + d * 128:
                                                  wo_c + wcol + (d + 1) * 128],
                                              asrc[:, d * 256:(d + 1) * 256],
                                              start=first, stop=last),
                                     incs=([(pe, 1)] if last else []))
                                nmm += 1
                        wof_last[fpar] = pe.n
                        settle_f(pb[fpar][:, 256:384])
                        w.wait("scalar", pe)
                        w.op("scalar", lambda e, fpar=fpar: e.copy(
                            o2[:, fpar * 256:(fpar + 1) * 256],
                            pb[fpar][:, 0:256]), incs=[(act, 1)])
                        bank_free[fpar] = (act, act.n)
                        twaits = []
                        for tloc in range(2):
                            pvs, pvc = pv_free[tb]
                            w.wait("tensor", pvs, pvc)
                            w.wait("tensor", act)
                            w.op("tensor", lambda e, tb=tb, fpar=fpar,
                                 tloc=tloc:
                                 e.transpose(pv[tb][:, 0:128],
                                             o2[:, fpar * 256 + tloc * 128:
                                                 fpar * 256 + (tloc + 1) * 128],
                                             id_f[:]), incs=[(pe, 1)])
                            twaits.append(tb)
                            tb = 1 - tb
                        settle_f(pv[twaits[1]][:, 256:384])
                        w.wait("vector", pe)
                        w.wait("vector", sm["dxb%d" % (sup % 2)],
                               xB_load[sup])
                        for tloc in range(2):
                            tbb = twaits[tloc]
                            w.op("vector", lambda e, tbb=tbb, tloc=tloc, f=f:
                                 e.scalar_tensor_tensor(
                                     xB[:, tloc * 1024 + f * 128:
                                        tloc * 1024 + (f + 1) * 128],
                                     pv[tbb][:, 0:128], 1.0,
                                     xB[:, tloc * 1024 + f * 128:
                                        tloc * 1024 + (f + 1) * 128],
                                     op0=ALU.mult, op1=ALU.add),
                                 incs=[(dve, 1)])
                        pv_free[0] = (dve, dve.n)
                        pv_free[1] = (dve, dve.n)
                    atn_free = pe.n
                    w.wait("sync", dve)
                    dstap = bass.AP(scr["xp"], sup * 2 * 128 * E,
                                    [[E, 128], [128 * E, 2], [1, E]])
                    srcap = bass.AP(xB, 0, [[2048, 128], [1024, 2], [1, 1024]])
                    w.dma("sync", dstap, srcap, sm["dxp"])
                    if sup + 1 < NT // 2:
                        issue_x_load(sup + 1)

        if dumpB:
            w.wait("gpsimd", dve)
            w.wait("gpsimd", act)
            w.wait("gpsimd", pe)
            for nm, buf, cols in (("dmp_psnap", psnap, 2048),
                                  ("dmp_gacc", gacc, 128),
                                  ("dmp_s", s_sb, 4096), ("dmp_phi", phi, 4096),
                                  ("dmp_plo", plo, 4096), ("dmp_atnh", atnh, 2048),
                                  ("dmp_atnl", atnl, 2048), ("dmp_scal", lnsc, 16),
                                  ("dmp_xb", xB, 2048), ("dmp_q", qbuf, 2048),
                                  ("dmp_k", kst_h, 16384), ("dmp_v", vst_h, 16384)):
                w.dma("gpsimd", bass.AP(dumpB_outs[nm], 0, [[cols, 128], [1, cols]]),
                      buf[:, 0:cols], sm["dy"])

        if "C" in phases:
            for engn in ("sync", "tensor", "vector", "scalar"):
                w.wait(engn, pe)
                w.wait(engn, dve)
                w.wait(engn, act)
                for sname in ("dxp", "dq0", "dq1", "dwo0", "dwo1",
                              "dxb0", "dxb1", "dgw", "dgq0", "dgq1"):
                    w.wait(engn, sm[sname])
            w.op("vector", lambda e: e.memset(onescol_b[:], 1.0),
                 incs=[(dve, 1)])
            for ec in range(NE):
                w.dma("sync", wqfc_sb[:, ec * HID:(ec + 1) * HID],
                      bass.AP(scr["wqfc"], ec * 128 * HID,
                              [[HID, 128], [1, HID]]), sm["dcw"])
            cbank_free = {i: (act, 0) for i in range(8)}
            pvbank_free = [(act, 0), (act, 0)]
            ctslot_free = [(act, 0), (dve, 0)]
            cxn_done = 0
            wpj_last = {0: 0, 1: 0}
            xp_stored = 0
            xp_loaded = {}

            def c_xload(s):
                par = s % 2
                w.wait("sync", sm["dy"])
                t0 = s * 2
                w.dma("sync", bass.AP(xpC, 0, [[2048, 128], [1024, 2], [1, 1024]]),
                      bass.AP(scr["xp"], t0 * 128 * E,
                              [[E, 128], [128 * E, 2], [1, E]]),
                      sm["dc%d" % par])
                xp_loaded[s] = sm["dc%d" % par].n

            c_xload(0)
            NS = NT // 2
            for s in range(NS):
                par = s % 2
                for tl in range(2):
                    xt = xpC[:, tl * 1024:(tl + 1) * 1024]
                    w.wait("vector", sm["dc%d" % par], xp_loaded[s])
                    w.wait("scalar", sm["dc%d" % par], xp_loaded[s])
                    (s1, s2, mu, e2c, msq, var, uu, sq, r0, e1, ee2, ee3, rr) = \
                        [lnsc[:, i:i + 1] for i in range(13)]
                    w.op("vector", lambda e, xt=xt: e.reduce_sum(s1, xt, AX.X))
                    w.op("scalar", lambda e, xt=xt: e.activation(
                        xsqC[:], xt, AF.Square, accum_out=s2), incs=[(act, 1)])
                    w.drain("vector")
                    w.op("vector", lambda e: e.tensor_scalar_mul(mu, s1, 1.0 / 1024.0))
                    w.wait("vector", act)
                    w.op("vector", lambda e: e.tensor_scalar_mul(e2c, s2, 1.0 / 1024.0))
                    w.drain("vector")
                    w.op("vector", lambda e: e.tensor_mul(msq, mu, mu))
                    w.drain("vector")
                    w.op("vector", lambda e: e.tensor_sub(var, e2c, msq))
                    w.drain("vector")
                    w.op("vector", lambda e: e.tensor_scalar_add(uu, var, 1e-5),
                         incs=[(dve, 1)])
                    w.wait("scalar", dve)
                    w.op("scalar", lambda e: e.activation(sq, uu, AF.Sqrt),
                         incs=[(act, 1)])
                    w.wait("vector", act)
                    w.op("vector", lambda e: e.reciprocal(r0, sq))
                    w.drain("vector")
                    w.op("vector", lambda e: e.tensor_mul(e1, r0, r0))
                    w.drain("vector")
                    w.op("vector", lambda e: e.tensor_mul(ee2, uu, e1))
                    w.drain("vector")
                    w.op("vector", lambda e: e.scalar_tensor_tensor(
                        ee3, ee2, -0.5, c15[:], op0=ALU.mult, op1=ALU.add))
                    w.drain("vector")
                    w.op("vector", lambda e: e.tensor_mul(rr, r0, ee3),
                         incs=[(dve, 1)])
                    w.wait("vector", dve)
                    w.wait("vector", pe, cxn_done)
                    w.op("vector", lambda e, xt=xt: e.tensor_scalar(
                        xcC[:], xt, mu, rr, op0=ALU.subtract, op1=ALU.mult),
                        incs=[(dve, 1)])
                    w.op("vector", lambda e: e.tensor_copy(xnhC[:], xcC[:]),
                         incs=[(dve, 1)])
                    w.op("vector", lambda e: e.scalar_tensor_tensor(
                        xnlC[:], xcC[:], 1.0, xnhC[:], op0=ALU.mult,
                        op1=ALU.subtract), incs=[(dve, 1)])
                    split_done = dve.n
                    pend = None

                    def cxn_evac(item):
                        i2, bk, dst = item
                        evn = "scalar" if bk == 0 else "vector"
                        evs = act if bk == 0 else dve
                        w.wait(evn, pe)
                        if evn == "scalar":
                            w.op("scalar", lambda e, dst=dst, bk=bk:
                                 e.copy(dst, pt[bk][:, 0:128]), incs=[(act, 1)])
                        else:
                            w.op("vector", lambda e, dst=dst, bk=bk:
                                 e.tensor_copy(dst, pt[bk][:, 0:128]),
                                 incs=[(dve, 1)])
                        ctslot_free[bk] = (evs, evs.n)

                    for i in range(16):
                        ec, half = i // 2, i % 2
                        srcb = xnhC if half == 0 else xnlC
                        dstb = xnThC if half == 0 else xnTlC
                        bk = i % 2
                        fs, fc_ = ctslot_free[bk]
                        w.wait("tensor", dve, split_done)
                        w.wait("tensor", fs, fc_)
                        w.op("tensor", lambda e, bk=bk, srcb=srcb, ec=ec:
                             e.transpose(pt[bk][:, 0:128],
                                         srcb[:, ec * 128:(ec + 1) * 128],
                                         id_b[:]), incs=[(pe, 1)])
                        if pend is not None:
                            cxn_evac(pend)
                        dst = dstb[:, ec * 256 + tl * 128:ec * 256 + tl * 128 + 128]
                        pend = (i, bk, dst)
                    settle_b(pend[1])
                    cxn_evac(pend)
                    cxn_done = pe.n

                w.wait("tensor", sm["dcw"])
                w.wait("tensor", act)
                w.wait("tensor", dve)
                for j in range(NJ):
                    bnk = j % 4
                    bs, bc = cbank_free[bnk]
                    w.wait("tensor", bs, bc)
                    nmm = 0
                    for xsrc in (xnThC, xnTlC):
                        for ec in range(NE):
                            first, last = nmm == 0, nmm == 15
                            w.op("tensor", lambda e, xsrc=xsrc, ec=ec, j=j,
                                 bnk=bnk, first=first, last=last:
                                 e.matmul(pbig[:, bnk * 512:bnk * 512 + 256],
                                          wqfc_sb[:, ec * HID + j * 128:
                                                  ec * HID + (j + 1) * 128],
                                          xsrc[:, ec * 256:(ec + 1) * 256],
                                          start=first, stop=last),
                                 incs=([(pe, 1)] if last else []))
                            nmm += 1
                    settle_b(0)
                    w.wait("scalar", pe)
                    w.op("scalar", lambda e, j=j, bnk=bnk: e.activation(
                        h_sb[:, j * 256:(j + 1) * 256],
                        pbig[:, bnk * 512:bnk * 512 + 256], AF.Gelu),
                        incs=[(act, 1)])
                    cbank_free[bnk] = (act, act.n)

                w.op("scalar", lambda e: e.copy(hb_sb[:], h_sb[:]),
                     incs=[(act, 1)])
                w.op("scalar", lambda e: e.activation(
                    hb2_sb[:], h_sb[:], AF.Square), incs=[(act, 1)])
                for b in range(2):
                    bs, bc = cbank_free[b]
                    w.wait("tensor", bs, bc)
                w.wait("tensor", act)
                w.wait("tensor", dve, tables_done)
                for which, (srcb, bnk) in enumerate(((hb_sb, 0), (hb2_sb, 1))):
                    for j in range(NJ):
                        first, last = j == 0, j == NJ - 1
                        w.op("tensor", lambda e, srcb=srcb, bnk=bnk, j=j,
                             first=first, last=last:
                             e.matmul(pbig[0:1, bnk * 512:bnk * 512 + 256],
                                      onescol_b[:],
                                      srcb[:, j * 256:(j + 1) * 256],
                                      start=first, stop=last),
                             incs=([(pe, 1)] if last else []))
                settle_b(0)
                s1r = o2c[0:1, 0:256]
                s2r = o2c[0:1, 256:512]
                mur = yC[0:1, 0:256]
                e2r = yC[0:1, 256:512]
                msr = yC[0:1, 512:768]
                varr = yC[0:1, 768:1024]
                uur = yC[0:1, 1024:1280]
                sqr = yC[0:1, 1280:1536]
                r0r = yC[0:1, 1536:1792]
                c15r = aA[0:1, 0:256]
                e1r = aB[0:1, 0:256]
                rrow = mubc[0:1, 0:256]
                w.wait("vector", pe)
                w.op("vector", lambda e: e.memset(c15r, 1.5))
                w.op("vector", lambda e: e.tensor_copy(s1r, pbig[0:1, 0:256]))
                w.op("vector", lambda e: e.tensor_copy(s2r, pbig[0:1, 512:768]))
                w.op("vector", lambda e: e.tensor_scalar_mul(mur, s1r, 1.0 / HID))
                w.op("vector", lambda e: e.tensor_scalar_mul(e2r, s2r, 1.0 / HID))
                w.op("vector", lambda e: e.tensor_mul(msr, mur, mur))
                w.op("vector", lambda e: e.tensor_sub(varr, e2r, msr))
                w.op("vector", lambda e: e.tensor_scalar_add(uur, varr, 1e-5),
                     incs=[(dve, 1)])
                w.wait("scalar", dve)
                w.op("scalar", lambda e: e.activation(sqr, uur, AF.Sqrt),
                     incs=[(act, 1)])
                w.wait("vector", act)
                w.op("vector", lambda e: e.reciprocal(r0r, sqr))
                w.op("vector", lambda e: e.tensor_mul(e1r, r0r, r0r))
                w.op("vector", lambda e: e.tensor_mul(e1r, uur, e1r))
                w.op("vector", lambda e: e.scalar_tensor_tensor(
                    e1r, e1r, -0.5, c15r, op0=ALU.mult, op1=ALU.add))
                w.op("vector", lambda e: e.tensor_mul(rrow, r0r, e1r),
                     incs=[(dve, 1)])
                w.wait("tensor", dve)
                for b in range(2, 4):
                    bs, bc = cbank_free[b]
                    w.wait("tensor", bs, bc)
                w.op("tensor", lambda e: e.matmul(
                    pbig[:, 1024:1280], onesrow[0:1, :], mur,
                    start=True, stop=True))
                w.op("tensor", lambda e: e.matmul(
                    pbig[:, 1536:1792], onesrow[0:1, :], rrow,
                    start=True, stop=True), incs=[(pe, 1)])
                settle_b(0)
                w.wait("vector", pe)
                w.op("vector", lambda e: e.tensor_copy(mubc[:], pbig[:, 1024:1280]))
                w.op("vector", lambda e: e.tensor_copy(rbc[:], pbig[:, 1536:1792]),
                     incs=[(dve, 1)])
                cbank_free[2] = (dve, dve.n)
                cbank_free[3] = (dve, dve.n)

                w.wait("vector", pe)
                for hc in range(NJ):
                    hsl = slice(hc * 256, (hc + 1) * 256)
                    w.op("vector", lambda e, hsl=hsl: e.tensor_sub(
                        aA[:], h_sb[:, hsl], mubc[:]))
                    w.op("vector", lambda e, hc=hc: e.scalar_tensor_tensor(
                        aB[:], aA[:], lnw_sb[:, hc:hc + 1], rbc[:],
                        op0=ALU.mult, op1=ALU.mult))
                    w.op("vector", lambda e, hsl=hsl, hc=hc: e.tensor_scalar(
                        hb_sb[:, hsl], aB[:], lnb_sb[:, hc:hc + 1], None,
                        op0=ALU.add), incs=[(dve, 1)])
                    w.op("vector", lambda e, hsl=hsl, hc=hc:
                         e.scalar_tensor_tensor(
                             hb2_sb[:, hsl], aB[:], lnb_sb[:, hc:hc + 1],
                             hb_sb[:, hsl], op0=ALU.add, op1=ALU.subtract),
                         incs=[(dve, 1)])
                xn2_done = dve.n

                for grp, dlist in ((0, (0, 1, 2, 3, 4, 5)), (1, (6, 7))):
                    accs = {}
                    for i2, d in enumerate(dlist):
                        if i2 < 4:
                            accs[d] = pbig[:, i2 * 512:i2 * 512 + 256]
                        else:
                            accs[d] = pv[i2 - 4][:, 0:256]
                    for i2, d in enumerate(dlist):
                        if i2 < 4:
                            bs, bc = cbank_free[i2]
                        else:
                            bs, bc = pvbank_free[i2 - 4]
                        w.wait("tensor", bs, bc)
                    w.wait("tensor", dve, xn2_done)
                    for hc in range(NJ):
                        hpar = hc % 2
                        w.wait("sync", pe, wpj_last[hpar])
                        ncols = 128 * len(dlist)
                        coff = 0 if grp == 0 else 768
                        w.dma("sync", wpj[:, hpar * 1024:hpar * 1024 + ncols],
                              bass.AP(scr["wqproj"], hc * 128 * E + coff,
                                      [[E, 128], [1, ncols]]),
                              sm["dpj%d" % hpar])
                        w.wait("tensor", sm["dpj%d" % hpar])
                        for i2, d in enumerate(dlist):
                            for ip, xsrc in enumerate((hb_sb, hb2_sb)):
                                first = hc == 0 and ip == 0
                                last = hc == NJ - 1 and ip == 1
                                fin = i2 == len(dlist) - 1 and ip == 1
                                w.op("tensor", lambda e, d=d, i2=i2, ip=ip,
                                     hc=hc, hpar=hpar, xsrc=xsrc, accs=accs,
                                     first=first, last=last:
                                     e.matmul(accs[d],
                                              wpj[:, hpar * 1024 + i2 * 128:
                                                  hpar * 1024 + (i2 + 1) * 128],
                                              xsrc[:, hc * 256:(hc + 1) * 256],
                                              start=first, stop=last),
                                     incs=([(pe, 1)] if fin else []))
                        wpj_last[hpar] = pe.n
                    settle_b(0)
                    w.wait("scalar", pe)
                    for i2, d in enumerate(dlist):
                        w.op("scalar", lambda e, d=d, i2=i2, accs=accs:
                             e.copy(yC[:, d * 256:(d + 1) * 256], accs[d]),
                             incs=[(act, 1)])
                        if i2 < 4:
                            cbank_free[i2] = (act, act.n)
                        else:
                            pvbank_free[i2 - 4] = (act, act.n)

                tb = 0
                tpend = []
                for d in range(NE):
                    for tl in range(2):
                        bs, bc = cbank_free[tb]
                        w.wait("tensor", bs, bc)
                        w.wait("tensor", act)
                        w.op("tensor", lambda e, tb=tb, d=d, tl=tl:
                             e.transpose(pbig[:, tb * 512:tb * 512 + 128],
                                         yC[:, d * 256 + tl * 128:
                                             d * 256 + (tl + 1) * 128],
                                         id_f[:]), incs=[(pe, 1)])
                        if tpend:
                            tb2, d2, tl2 = tpend.pop(0)
                            w.wait("vector", pe)
                            w.op("vector", lambda e, tb2=tb2, d2=d2, tl2=tl2:
                                 e.scalar_tensor_tensor(
                                     xpC[:, tl2 * 1024 + d2 * 128:
                                         tl2 * 1024 + (d2 + 1) * 128],
                                     pbig[:, tb2 * 512:tb2 * 512 + 128], 1.0,
                                     xpC[:, tl2 * 1024 + d2 * 128:
                                         tl2 * 1024 + (d2 + 1) * 128],
                                     op0=ALU.mult, op1=ALU.add),
                                 incs=[(dve, 1)])
                            cbank_free[tb2] = (dve, dve.n)
                        tpend.append((tb, d, tl))
                        tb = (tb + 1) % 4
                settle_b(0)
                tb2, d2, tl2 = tpend.pop(0)
                w.wait("vector", pe)
                w.op("vector", lambda e, tb2=tb2, d2=d2, tl2=tl2:
                     e.scalar_tensor_tensor(
                         xpC[:, tl2 * 1024 + d2 * 128:
                             tl2 * 1024 + (d2 + 1) * 128],
                         pbig[:, tb2 * 512:tb2 * 512 + 128], 1.0,
                         xpC[:, tl2 * 1024 + d2 * 128:
                             tl2 * 1024 + (d2 + 1) * 128],
                         op0=ALU.mult, op1=ALU.add), incs=[(dve, 1)])
                cbank_free[tb2] = (dve, dve.n)
                w.wait("sync", dve)
                w.dma("sync", bass.AP(y_out, s * 2 * 128 * E,
                                      [[E, 128], [128 * E, 2], [1, E]]),
                      bass.AP(xpC, 0, [[2048, 128], [1024, 2], [1, 1024]]),
                      sm["dy"])
                if s + 1 < NS:
                    c_xload(s + 1)

        if dumpC:
            w.wait("gpsimd", dve)
            w.wait("gpsimd", act)
            w.wait("gpsimd", pe)
            for nm, buf, cols in (("dmc_h", h_sb, 8192), ("dmc_x2h", hb_sb, 8192),
                                  ("dmc_x2l", hb2_sb, 8192), ("dmc_mu", mubc, 256),
                                  ("dmc_rb", rbc, 256), ("dmc_xnt", xnThC, 2048),
                                  ("dmc_xntl", xnTlC, 2048), ("dmc_y", yC, 2048)):
                w.dma("gpsimd", bass.AP(dumpC_outs[nm], 0, [[cols, 128], [1, cols]]),
                      buf[:, 0:cols], sm["dy"])

        for name, dout in dbg_outs.items():
            t = scr[name]
            n = 1
            for d_ in t.shape:
                n *= d_
            w.wait("sync", sm["dscr"])
            w.wait("sync", sm["dgw"])
            chunk = 16384
            w.dma("sync", bass.AP(dout, 0, [[chunk, n // chunk], [1, chunk]]),
                  bass.AP(t, 0, [[chunk, n // chunk], [1, chunk]]), sm["dy"])

        if dbg_probe2 is not None:
            w.wait("gpsimd", dve)
            w.dma("gpsimd", bass.AP(dbg_probe2, 0, [[2048, 128], [1, 2048]]),
                  snap_sb[:], sm["dy"])
            w.dma("gpsimd", bass.AP(dbg_probe2, 128 * 2048, [[32, 128], [1, 32]]),
                  snap2_sb[:], sm["dy"])
        w.wait("sync", sm["dy"])
        w.wait("sync", sm["dscr"])
        w.wait("sync", sm["dgw"])

    with es:
        with nc.Block() as block:
            def make_stream(name):
                def _stream(eng_obj):
                    sems = {k: Sem(sem_hs[k], k) for k in sem_names}
                    w = Walk(name,
                             {"sync": nc.sync, "tensor": nc.tensor,
                              "vector": nc.vector, "scalar": nc.scalar,
                              "gpsimd": nc.gpsimd}, sems)
                    emit_program(w, sems)
                return _stream

            block.sync(make_stream("sync"))
            block.tensor(make_stream("tensor"))
            block.vector(make_stream("vector"))
            block.scalar(make_stream("scalar"))
            block.gpsimd(make_stream("gpsimd"))
        esAB.close()

    return nc



def _perm_qkv():
    perm = np.zeros(3 * E, dtype=np.int64)
    for g in range(16):
        mp, j = g // 4, g % 4
        base = mp * 768
        for d in range(64):
            perm[g * 64 + d] = base + j * 64 + d
            perm[1024 + g * 64 + d] = base + 512 + j * 64 + d
            perm[2048 + g * 64 + d] = base + 256 + j * 64 + d
    return perm


def _tables():
    pos = np.arange(S, dtype=np.float64)
    inv = 10000.0 ** (-np.arange(0, 32, 2, dtype=np.float64) / 32.0)
    ang = np.outer(inv, pos)
    c = np.cos(ang)
    s_ = np.sin(ang)
    cosW = np.ones((128, S), dtype=np.float64)
    sinW = np.zeros((128, S), dtype=np.float64)
    for blk in (0, 64):
        for d in range(32):
            i = d // 2
            cosW[blk + d] = c[i]
            sinW[blk + d] = (-s_[i]) if d % 2 == 0 else s_[i]
    ch = cosW.astype(np.float32)
    cl = (cosW - ch.astype(np.float64)).astype(np.float32)
    sh = sinW.astype(np.float32)
    sl = (sinW - sh.astype(np.float64)).astype(np.float32)
    return ch, cl, sh, sl


def _mask_master():
    m = np.zeros((128, 1024), dtype=np.float32)
    q = np.arange(128)[:, None]
    mm = np.arange(1024)[None, :]
    m[(mm > q + 512)] = np.float32(MASKV)
    return m


def host_inputs(x_full, w_qkv, w_out, W_fc, ln_w, ln_b, W_proj):
    perm = _perm_qkv()
    wqkvT_a = np.ascontiguousarray(np.asarray(w_qkv)[perm].T)
    woutT_a = np.ascontiguousarray(np.asarray(w_out).T)
    wfcT_a = np.ascontiguousarray(np.asarray(W_fc).T)
    wprojT_a = np.ascontiguousarray(np.asarray(W_proj).T)
    ch, cl, sh, sl = _tables()
    common = {
        "wqkvT": wqkvT_a, "woutT": woutT_a, "wfcT": wfcT_a, "wprojT": wprojT_a,
        "ln_w": np.ascontiguousarray(np.asarray(ln_w, dtype=np.float32)),
        "ln_b": np.ascontiguousarray(np.asarray(ln_b, dtype=np.float32)),
        "cos_hi": ch, "cos_lo": cl, "sin_hi": sh, "sin_lo": sl,
        "ident_b": np.eye(128).astype(ml_dtypes.bfloat16),
        "ident_f": np.eye(128, dtype=np.float32),
        "mask_hi": _mask_master().astype(ml_dtypes.bfloat16),
    }
    in_maps = []
    for b in range(8):
        m = dict(common)
        m["x"] = np.ascontiguousarray(np.asarray(x_full)[b], dtype=np.float32)
        in_maps.append(m)
    return in_maps


def kernel(L0_attn_hidden_states, L0_attn_w_qkv, L0_attn_w_out,
           L0_ffn_W_fc, L0_ffn_ln_w, L0_ffn_ln_b, L0_ffn_W_proj):
    in_maps = host_inputs(L0_attn_hidden_states, L0_attn_w_qkv, L0_attn_w_out,
                          L0_ffn_W_fc, L0_ffn_ln_w, L0_ffn_ln_b, L0_ffn_W_proj)
    nc = build()
    res = run_bass_kernel_spmd(nc, in_maps, core_ids=list(range(8)))
    return np.stack([np.asarray(r["y"], dtype=np.float32)
                     for r in res.results], axis=0)
